# version 11 (final)
# speedup vs baseline: 1.2965x; 1.0220x over previous
import os
import math
import numpy as np
import ml_dtypes
import concourse.bass as bass
import concourse.mybir as mybir
from concourse.bass_utils import run_bass_kernel_spmd
from concourse.aot_env import allow_ici_all_to_all
from contextlib import ExitStack

F32 = mybir.dt.float32
BF16 = mybir.dt.bfloat16
AF = mybir.ActivationFunctionType
ALU = mybir.AluOpType

BF = ml_dtypes.bfloat16

B, S, D = 2, 4096, 2048
H, HD = 16, 128
I = 8192
NC = 8
T = 1024
NTC = 8
EPS = 1e-5
RSQ_D = 1.0 / math.sqrt(2048.0)
RSQ_I = 1.0 / math.sqrt(8192.0)
RSQ_HD = 1.0 / math.sqrt(128.0)
NEG = -1.0e30

SLOPES = [2.0 ** (-0.5 * (h + 1)) for h in range(H)]
CUT = [int(118 * 16), 10 ** 9]

DBG = os.environ.get("MKDBG", "") == "1"
PH = int(os.environ.get("MKPHASE", "9"))
MKFFN = os.environ.get("MKFFN", "")
MKAG = os.environ.get("MKAG", "")


def hadamard(n):
    h = np.ones((1, 1), dtype=np.float32)
    while h.shape[0] < n:
        h = np.block([[h, h], [h, -h]])
    return h


def kb_range(qt, slot):
    q0 = qt * 512
    hi = (q0 + 511) // 128
    lo = 0
    while q0 - (128 * lo + 127) > CUT[slot]:
        lo += 1
    return lo, hi


ENGS = ("tensor", "vector", "scalar", "gpsimd", "sync")


class Sem:
    def __init__(self, nc, ctx, name):
        self.h = ctx.enter_context(nc.semaphore(name))
        self.n = 0


class Prog:
    def __init__(self):
        self.ops = {e: [] for e in ENGS}

    def op(self, eng, fn):
        self.ops[eng].append(fn)

    def opi(self, eng, sem, fn, k=1):
        h = sem.h
        self.ops[eng].append(lambda E: fn(E).then_inc(h, k))
        sem.n += k
        return sem.n

    def dma(self, eng, sem, out, in_, **kw):
        h = sem.h
        self.ops[eng].append(lambda E: E.dma_start(out=out, in_=in_, **kw).then_inc(h, 16))
        sem.n += 16
        return sem.n

    def wait(self, eng, sem, tok):
        if tok <= 0:
            return
        h = sem.h
        self.ops[eng].append(lambda E: E.wait_ge(h, tok))

    def replay(self, block):
        ops = self.ops

        @block.tensor
        def _(E):
            for f in ops["tensor"]:
                f(E)

        @block.vector
        def _(E):
            for f in ops["vector"]:
                f(E)

        @block.scalar
        def _(E):
            for f in ops["scalar"]:
                f(E)

        @block.gpsimd
        def _(E):
            for f in ops["gpsimd"]:
                f(E)

        @block.sync
        def _(E):
            for f in ops["sync"]:
                f(E)


def build_nc():
    nc = bass.Bass("TRN2", target_bir_lowering=False)

    x_in = nc.dram_tensor("x_loc", [T, D], F32, kind="ExternalInput")
    wq_d = nc.dram_tensor("wq", [D, D], F32, kind="ExternalInput")
    wk_d = nc.dram_tensor("wk", [D, D], F32, kind="ExternalInput")
    wv_d = nc.dram_tensor("wv", [D, D], F32, kind="ExternalInput")
    wo_d = nc.dram_tensor("wo", [D, D], F32, kind="ExternalInput")
    wup_d = nc.dram_tensor("wup_loc", [I // NC, D], F32, kind="ExternalInput")
    wgate_d = nc.dram_tensor("wgate_loc", [I // NC, D], F32, kind="ExternalInput")
    wdown_d = nc.dram_tensor("wdown_loc", [D // NC, I], F32, kind="ExternalInput")
    bq_d = nc.dram_tensor("bq", [1, D], F32, kind="ExternalInput")
    bk_d = nc.dram_tensor("bk", [1, D], F32, kind="ExternalInput")
    bo_d = nc.dram_tensor("bo", [1, D], F32, kind="ExternalInput")
    bvT_d = nc.dram_tensor("bvT", [128, 2], F32, kind="ExternalInput")
    slopes_d = nc.dram_tensor("slopes_b", [128, 2], F32, kind="ExternalInput")
    su_d = nc.dram_tensor("su_ug", [1, D], F32, kind="ExternalInput")
    sudp_d = nc.dram_tensor("su_dp", [1, I], F32, kind="ExternalInput")
    wsu_d = nc.dram_tensor("wsu", [1, I], F32, kind="ExternalInput")
    wsg_d = nc.dram_tensor("wsg", [1, I], F32, kind="ExternalInput")
    wsd_d = nc.dram_tensor("wsd", [1, D], F32, kind="ExternalInput")

    out_d = nc.dram_tensor("out_loc", [T, D], F32, kind="ExternalOutput")
    dbuf_d = nc.dram_tensor("dbg_buf", [128, 16384], F32, kind="ExternalOutput") if (MKFFN or MKAG) else None
    if DBG:
        dbg1 = nc.dram_tensor("dbg_x1", [T, D], F32, kind="ExternalOutput")
        dbg2 = nc.dram_tensor("dbg_x2", [T, D], F32, kind="ExternalOutput")

    x_spill = nc.dram_tensor("x_spill", [T, D], F32, kind="Internal")
    PR_UP = 4 * 16 * 128 * 256
    PR_DN = 64 * 128 * 256
    PR_RANK = 2 * PR_UP + PR_DN
    ag_in = nc.dram_tensor("ag_in", [2 * PR_UP], BF16, kind="Internal")
    ag_out = nc.dram_tensor("ag_out", [NC * 2 * PR_UP], BF16, kind="Internal")
    agd_in = nc.dram_tensor("agd_in", [PR_DN], BF16, kind="Internal")
    agd_out = nc.dram_tensor("agd_out", [NC * PR_DN], BF16, kind="Internal")

    SH_K = 2 * 128 * 1024
    SH_Q = 2 * 128 * 1024
    SH_V = 2 * 8 * 128 * 128
    SH_SZ = SH_K + SH_Q + SH_V
    a1_in = nc.dram_tensor("a1_in", [NC * SH_SZ], BF16, kind="Internal")
    a1_out = nc.dram_tensor("a1_out", [NC * SH_SZ], BF16, kind="Internal")
    SH2 = 2 * 128 * 1024
    a2_in = nc.dram_tensor("a2_in", [NC * SH2], BF16, kind="Internal")
    a2_out = nc.dram_tensor("a2_out", [NC * SH2], BF16, kind="Internal")

    ident_c = nc.inline_tensor(np.eye(128, dtype=BF), name="c_ident")
    h128_c = nc.inline_tensor(hadamard(128).astype(BF), name="c_h128")
    ki = np.arange(128, dtype=np.float32)[:, None]
    qj = np.arange(512, dtype=np.float32)[None, :]
    D_c = nc.inline_tensor((ki - qj).astype(np.float32), name="c_D")
    m4 = np.concatenate([np.where(ki + 128.0 * mm > qj, NEG, 0.0).astype(np.float32)
                         for mm in (3, 2, 1, 0)], axis=1)
    M4_c = nc.inline_tensor(m4, name="c_M4")
    drow = np.broadcast_to((-128.0 * (np.arange(32, dtype=np.float32) - 3))[None, :],
                           (128, 32)).copy()
    drow_c = nc.inline_tensor(drow, name="c_drow")

    ctx = ExitStack()
    with ctx:
        alloc = nc.alloc_sbuf_tensor_at
        ARENA_SZ = 200 * 1024
        arena = nc.alloc_sbuf_tensor("arena", [128, ARENA_SZ], mybir.dt.uint8, side="left")
        SBUF_BASE = nc.lookup_mloc(arena).addr
        off = SBUF_BASE

        def at(name, shape, dtype, align=32):
            nonlocal off
            off = (off + align - 1) // align * align
            t = alloc(name, shape, dtype, offset=off)
            w = 1
            for s_ in shape[1:]:
                w *= s_
            off += w * mybir.dt.size(dtype)
            return t

        ident = at("ident", [128, 128], BF16)
        h128 = at("h128", [128, 128], BF16)
        ones_bf = at("ones_bf", [128, 1], BF16)
        ones_f = at("ones_f", [1, 128], F32)
        D_t = at("D_t", [128, 512], F32)
        drow_t = at("drow", [128, 32], F32)
        slopes_t = at("slopes", [128, 2], F32)
        bvT_t = at("bvT_t", [128, 2], F32)
        su_fm = at("su_fm", [128, 16], F32)
        bq_fm = at("bq_fm", [128, 16], F32)
        bk_fm = at("bk_fm", [128, 16], F32)
        sg_fm = at("sg_fm", [128, 64], F32)
        cu_fm = at("cu_fm", [128, 64], F32)
        wsd_fm = at("wsd_fm", [128, 16], F32)
        stats = at("stats", [128, 24], F32)
        mv = at("mv", [128, 2], F32)
        rstd = at("rstd", [128, 1], F32)
        vec_tmp = at("vec_tmp", [128, 64], F32)
        silu_b = [at(f"silu{i}", [128, 512], F32) for i in range(2)]
        rzb = at("rzb", [128, 512], F32)
        assert off <= SBUF_BASE + 14 * 1024, off

        A0 = SBUF_BASE + 14 * 1024
        ASZ = 64 * 1024
        off = A0
        y1 = at("y1", [128, 16 * 1024], BF16)
        wub = [at(f"wub{i}", [128, 16 * 256], BF16) for i in range(2)]
        wgb = [at(f"wgb{i}", [128, 16 * 256], BF16) for i in range(2)]
        assert off - A0 <= ASZ, off - A0
        off = A0
        down_fm = at("down_fm", [128, 16 * 1024], BF16)
        wdb = [at(f"wdb{i}", [128, 32 * 256], BF16) for i in range(2)]
        assert off - A0 <= ASZ
        off = A0
        M4_t = at("M4_t", [128, 4 * 512], F32)
        Du = at("Du", [128, 5 * 512], F32)
        biasv = at("biasv", [128, 32], F32)
        k_sb = [at(f"k_sb{i}", [128, 1024], BF16) for i in range(6)]
        v_sb = [at(f"v_sb{i}", [128, 1024], BF16) for i in range(6)]
        q_sb = [at(f"q_sb{i}", [128, 512], BF16) for i in range(2)]
        tmpS = [at(f"tmpS{i}", [128, 512], F32) for i in range(4)]
        abuf = [at(f"abuf{i}", [128, 512], BF16) for i in range(4)]
        z_sb = at("z_sb", [1, 512], F32)
        rz_sb = at("rz_sb", [1, 512], F32)
        qkv_ev = [at(f"qkv_ev{i}", [128, 512], BF16) for i in range(2)]
        assert off - A0 <= ASZ, off - A0
        off = A0
        p_wrow = [at(f"p_wrow{i}", [128, 8192], BF16) for i in range(2)]
        p_out = [at(f"p_out{i}", [128, 512], BF16) for i in range(8)]
        assert off - A0 <= ASZ

        B0 = A0 + ASZ
        BSZ = 128 * 1024
        off = B0
        htil = at("htil", [128, 64 * 1024], BF16)
        off = B0
        x_tm = at("x_tm", [128, 8 * 2048], F32)
        off = B0
        y0 = at("y0", [128, 16 * 1024], F32)
        off = B0 + 64 * 1024
        xln = at("xln", [128, 8 * 2048], BF16)
        off = B0 + 64 * 1024
        wvb = [at(f"wvb{i}", [128, 16 * 512], BF16) for i in range(2)]
        assert off - B0 <= 96 * 1024
        off = B0 + 64 * 1024
        wqb = [at(f"wqb{i}", [128, 16 * 128], BF16) for i in range(2)]
        off = B0 + 64 * 1024
        wob = [at(f"wob{i}", [128, 16 * 512], BF16) for i in range(2)]
        off = B0 + 96 * 1024
        btemp = at("btemp", [128, 8 * 1024], F32)
        off = B0 + 96 * 1024
        xln2 = at("xln2", [128, 16 * 1024], BF16)
        off = B0 + 96 * 1024
        ogath = at("ogath", [128, 16 * 1024], BF16)
        off = B0 + 96 * 1024
        bo_b = at("bo_b", [128, 2048], F32)
        off = B0
        pp_wT = at("pp_wT", [128, 16 * 1024], BF16)
        pp_wTd = at("pp_wTd", [128, 64 * 128], BF16)
        pp_wH = at("pp_wH", [128, 64 * 128], F32)
        pp_pp = at("pp_pp", [128, 32 * 128], F32)
        pp_ob = at("pp_ob", [128, 64 * 128], BF16)
        assert off - B0 <= BSZ

        pb = [ctx.enter_context(nc.psum_tensor(f"pb{i}", [128, 512], F32)) for i in range(6)]
        pbT = [ctx.enter_context(nc.psum_tensor(f"pbT{i}", [128, 1024], BF16)) for i in range(2)]

        sLD = Sem(nc, ctx, "sLD")
        sLG = Sem(nc, ctx, "sLG")
        sX = Sem(nc, ctx, "sX")
        sWU = Sem(nc, ctx, "sWU")
        sWD = Sem(nc, ctx, "sWD")
        sKV = Sem(nc, ctx, "sKV")
        sQL = Sem(nc, ctx, "sQL")
        sOG = Sem(nc, ctx, "sOG")
        sWR = Sem(nc, ctx, "sWR")
        sT = Sem(nc, ctx, "sT")
        sV = Sem(nc, ctx, "sV")
        sS = Sem(nc, ctx, "sS")
        sCC = Sem(nc, ctx, "sCC")

        P = Prog()

        bank_state = {}

        def bank_acquire(i):
            st = bank_state.get(i)
            if st is not None:
                P.wait("tensor", st[0], st[1])

        def bank_release(i, sem, tok):
            bank_state[i] = (sem, tok)

        pbT_state = {}

        def sync_load(out, in_, **kw):
            return P.dma("sync", sLD, out, in_, **kw)

        def cast_load(out, in_, **kw):
            return P.dma("gpsimd", sLG, out, in_, **kw)

        def transpose_burst(srcs, evac_fn, burst=8):
            tok = 0
            nb = (len(srcs) + burst - 1) // burst
            for bi in range(nb):
                bank = bi % 2
                chunk = srcs[bi * burst:(bi + 1) * burst]
                stt = pbT_state.get(("b", bank))
                if stt is not None:
                    P.wait("tensor", sV, stt)
                for j, s_ in enumerate(chunk):
                    P.opi("tensor", sT, lambda t, s_=s_, bank=bank, j=j: t.transpose(
                        pbT[bank][:, 128 * j:128 * (j + 1)], s_, ident[:]))
                tt = sT.n
                P.wait("vector", sT, tt)
                ev = P.opi("vector", sV, lambda v, bank=bank, b0=bi * burst, n=len(chunk):
                           evac_fn(v, pbT[bank][:, 0:128 * n], b0, n))
                pbT_state[("b", bank)] = ev
                tok = ev
            return tok

        def transpose_seq(items):
            toks = []
            for k, (src, evac) in enumerate(items):
                slot = k % 2
                ps = pbT[slot][:, 0:128]
                stt = pbT_state.get(slot)
                if stt is not None:
                    P.wait("tensor", sV, stt)
                tt = P.opi("tensor", sT, lambda t, src=src, ps=ps: t.transpose(ps, src, ident[:]))
                P.wait("vector", sT, tt)
                ev = P.opi("vector", sV, lambda v, evac=evac, ps=ps: evac(v, ps))
                pbT_state[slot] = ev
                toks.append(ev)
            return toks[-1] if toks else 0

        tok0 = 0
        for dst, src in ((ident, ident_c), (h128, h128_c), (D_t, D_c),
                         (drow_t, drow_c), (slopes_t, slopes_d), (bvT_t, bvT_d)):
            tok0 = sync_load(dst[:], src[:])

        def fm_load(dst, src, n):
            return sync_load(dst[:, 0:n], src[0, :].rearrange("(n p) -> p n", p=128),
                             allow_slow_non_contiguous=True)

        fm_load(su_fm, su_d, 16)
        fm_load(bq_fm, bq_d, 16)
        fm_load(bk_fm, bk_d, 16)
        fm_load(wsd_fm, wsd_d, 16)
        fm_load(sg_fm, wsg_d, 64)
        fm_load(cu_fm, wsu_d, 64)
        tok_vecs = fm_load(vec_tmp, sudp_d, 64)
        P.wait("vector", sLD, tok_vecs)
        P.op("vector", lambda v: v.tensor_tensor(cu_fm[:], cu_fm[:], vec_tmp[:], ALU.mult))
        P.op("vector", lambda v: v.tensor_scalar(cu_fm[:], cu_fm[:], RSQ_D, None, ALU.mult))
        P.op("vector", lambda v: v.tensor_scalar(sg_fm[:], sg_fm[:], RSQ_D, None, ALU.mult))
        P.op("vector", lambda v: v.tensor_scalar(wsd_fm[:], wsd_fm[:], RSQ_I, None, ALU.mult))
        P.op("vector", lambda v: v.memset(ones_bf[:], 1.0))
        P.op("vector", lambda v: v.memset(ones_f[:], 1.0))

        pw_free = {}

        def prep_upgate(w_dram, blob_off):
            ev_last = 0
            for sub in range(8):
                pwb = sub % 2
                if ("pw", pwb) in pw_free:
                    P.wait("gpsimd", sT, pw_free[("pw", pwb)])
                tl = cast_load(p_wrow[pwb][:, 0:2048], w_dram[128 * sub:128 * (sub + 1), :])
                P.wait("tensor", sLG, tl)
                P.wait("tensor", sLD, tok0)
                srcs = [p_wrow[pwb][:, 128 * fc:128 * (fc + 1)] for fc in range(16)]

                def evup(v, ps, b0, n, sub=sub):
                    dst = pp_wT[:].rearrange("p (f m) -> p f m", f=16)[
                        :, b0:b0 + n, 128 * sub:128 * (sub + 1)]
                    return v.tensor_copy(dst, ps.rearrange("p (f m) -> p f m", f=n))
                ev_last = transpose_burst(srcs, evup)
                pw_free[("pw", pwb)] = sT.n
            if MKAG == "t0":
                P.wait("gpsimd", sV, ev_last)
                dt_ = P.dma("gpsimd", sLG, dbuf_d[:], pp_wT[:])
                P.wait("gpsimd", sLG, dt_)
                return "stop"
            if MKAG == "i0":
                P.wait("gpsimd", sV, ev_last)
                dt_ = P.dma("gpsimd", sLG, dbuf_d[:, 0:128], ident[:])
                dt_ = P.dma("gpsimd", sLG, dbuf_d[:, 128:256], h128[:])
                dt_ = P.dma("gpsimd", sLG, dbuf_d[:, 256:2304], p_wrow[0][:, 0:2048])
                P.wait("gpsimd", sLG, dt_)
                return "stop"
            P.wait("tensor", sV, ev_last)
            wring = {}
            k_ = 0
            for fc in range(16):
                for m2 in range(4):
                    bi = m2 % 2
                    bank_acquire(bi)
                    tt = P.opi("tensor", sT, lambda t, fc=fc, m2=m2, bi=bi: t.matmul(
                        pb[bi][:, 0:256], h128[:],
                        pp_wT[:, 1024 * fc + 256 * m2:1024 * fc + 256 * (m2 + 1)],
                        start=True, stop=True))
                    P.wait("vector", sT, tt)
                    rs = k_ % 8
                    if rs in wring:
                        P.wait("vector", sLD, wring[rs])
                    ev = P.opi("vector", sV, lambda v, bi=bi, rs=rs: v.tensor_copy(
                        p_out[rs][:, 0:256], pb[bi][:, 0:256]))
                    bank_release(bi, sV, ev)
                    boff = blob_off + (m2 * 16 + fc) * 128 * 256
                    P.wait("sync", sV, ev)
                    wtok = sync_load(
                        ag_in.ap()[boff:boff + 128 * 256].rearrange("(p m) -> p m", p=128),
                        p_out[rs][:, 0:256])
                    wring[rs] = wtok
                    k_ += 1

        P.wait("vector", sLD, tok0)
        if PH >= 1:
            rv = prep_upgate(wup_d, 0)
            if rv == "stop":
                with nc.Block() as blk:
                    P.replay(blk)
                return nc
        if PH >= 1 and os.environ.get("MKSUB", "9") >= "2":
            prep_upgate(wgate_d, PR_UP)
        tok_ub = sLD.n
        P.wait("gpsimd", sLD, tok_ub)
        tok_ag = P.opi("gpsimd", sCC, lambda g: g.collective_compute(
            "AllGather", ALU.bypass, replica_groups=[list(range(NC))],
            ins=[ag_in.ap().opt()], outs=[ag_out.ap().opt()]))

        ag3 = agd_in.ap().rearrange("(g p m) -> g p m", g=64, p=128)
        for half in (range(2) if (PH >= 1 and os.environ.get("MKSUB", "9") >= "3") else ()):
            P.wait("gpsimd", sT, sT.n)
            tl = cast_load(p_wrow[half][:, 0:8192], wdown_d[128 * half:128 * (half + 1), :])
            P.wait("tensor", sLG, tl)
            srcs = [p_wrow[half][:, 128 * gc:128 * (gc + 1)] for gc in range(64)]

            def evdn(v, ps, b0, n):
                return v.tensor_copy(pp_wTd[:, 128 * b0:128 * (b0 + n)], ps)
            ev_last_d = transpose_burst(srcs, evdn)
            P.wait("tensor", sV, ev_last_d)
            for gc in range(64):
                bi = gc % 2
                bank_acquire(bi)
                tt = P.opi("tensor", sT, lambda t, gc=gc, bi=bi: t.matmul(
                    pb[bi][:, 0:128], h128[:], pp_wTd[:, 128 * gc:128 * (gc + 1)],
                    start=True, stop=True))
                P.wait("vector", sT, tt)
                ev = P.opi("vector", sV, lambda v, gc=gc, bi=bi: v.tensor_copy(
                    pp_wH[:, 128 * gc:128 * (gc + 1)], pb[bi][:, 0:128]))
                bank_release(bi, sV, ev)
            for st_ in range(6):
                step = 1 << st_
                a_cnt = 64 // (2 * step)
                wx = pp_wH[:].rearrange("p (a s b m) -> p a s b m", a=a_cnt, s=2, b=step)
                tx = pp_pp[:].rearrange("p (a b m) -> p a b m", a=a_cnt, b=step)
                P.op("vector", lambda v, wx=wx, tx=tx: v.tensor_sub(
                    tx[:, :, :, :], wx[:, :, 0, :, :], wx[:, :, 1, :, :]))
                P.op("vector", lambda v, wx=wx: v.tensor_add(
                    wx[:, :, 0, :, :], wx[:, :, 0, :, :], wx[:, :, 1, :, :]))
                P.op("vector", lambda v, wx=wx, tx=tx: v.tensor_copy(
                    wx[:, :, 1, :, :], tx[:, :, :, :]))
            ev = P.opi("vector", sV, lambda v: v.tensor_copy(pp_ob[:], pp_wH[:]))
            P.wait("sync", sV, ev)
            wtok = 0
            for gc in range(64):
                wtok = sync_load(ag3[gc, :, 128 * half:128 * (half + 1)],
                                 pp_ob[:, 128 * gc:128 * (gc + 1)])
            P.wait("vector", sLD, wtok)
            P.wait("tensor", sLD, wtok)

        tok_blob = sLD.n
        P.wait("gpsimd", sLD, tok_blob)
        if PH < 1 or os.environ.get("MKSUB", "9") < "4":
            tok_agd = P.opi("gpsimd", sCC, lambda g: g.engine_nop())
        else:
          tok_agd = P.opi("gpsimd", sCC, lambda g: g.collective_compute(
            "AllGather", ALU.bypass, replica_groups=[list(range(NC))],
            ins=[agd_in.ap().opt()], outs=[agd_out.ap().opt()]))

        agw = ag_out.ap().rearrange("(r e) -> r e", r=NC)
        agdw = agd_out.ap().rearrange("(r e) -> r e", r=NC)

        def up_tile_view(rank, m2c, isgate):
            base = (PR_UP if isgate else 0) + m2c * (16 * 128 * 256)
            return agw[rank, base:base + 16 * 128 * 256].rearrange(
                "(f p m) -> p f m", f=16, p=128)

        def down_tile_view(rank, ghalf):
            base = ghalf * (32 * 128 * 256)
            return agdw[rank, base:base + 32 * 128 * 256].rearrange(
                "(g p m) -> p g m", g=32, p=128)

        if MKAG:
            rr = int(MKAG[1])
            if MKAG[0] == "u":
                srcv = agw[rr, 0:PR_UP].rearrange("(a p m) -> p a m", a=64, p=128)
            elif MKAG[0] == "g":
                srcv = agw[rr, PR_UP:2 * PR_UP].rearrange("(a p m) -> p a m", a=64, p=128)
            else:
                srcv = agw[rr, 2 * PR_UP:2 * PR_UP + 128 * 16384].rearrange(
                    "(a p m) -> p a m", a=64, p=128)
            P.wait("gpsimd", sCC, tok_ag)
            dt_ = P.dma("gpsimd", sLG, dbuf_d[:].rearrange("p (a m) -> p a m", a=64), srcv)
            P.wait("gpsimd", sLG, dt_)
            with nc.Block() as blk:
                P.replay(blk)
            return nc

        tok_x_holder = [0]
        for tc in range(NTC):
            tok_x_holder[0] = P.dma("sync", sX, x_tm[:, 2048 * tc:2048 * (tc + 1)],
                                    x_in[128 * tc:128 * (tc + 1), :])

        def emit_ln(xready_tok, xready_sem):
            for tc in range(NTC):
                xv = x_tm[:, 2048 * tc:2048 * (tc + 1)]
                P.wait("vector", xready_sem, xready_tok)
                for seg in range(3):
                    P.op("vector", lambda v, xv=xv, seg=seg: v.bn_stats(
                        stats[:, 6 * seg:6 * seg + 6], xv[:, 512 * seg:512 * (seg + 1)]))
                bt = P.opi("vector", sV, lambda v, xv=xv: v.bn_stats(
                    stats[:, 18:24], xv[:, 1536:2048]))
                P.wait("vector", sV, bt)
                ba = P.opi("vector", sV, lambda v: v.bn_aggr(mv[:], stats[:]))
                P.wait("vector", sV, ba)
                va = P.opi("vector", sV, lambda v: v.tensor_scalar(
                    rstd[:], mv[:, 1:2], EPS, None, ALU.add))
                P.wait("scalar", sV, va)
                sq = P.opi("scalar", sS, lambda s: s.activation(rstd[:], rstd[:], AF.Sqrt))
                P.wait("vector", sS, sq)
                P.op("vector", lambda v: v.reciprocal(rstd[:], rstd[:]))
                P.op("vector", lambda v, tc=tc, xv=xv: v.tensor_scalar(
                    xln[:, 2048 * tc:2048 * (tc + 1)], xv, mv[:, 0:1], rstd[:],
                    ALU.subtract, ALU.mult))
            return P.opi("vector", sV, lambda v: v.engine_nop())

        def emit_ffn(call_i):
            v0 = sV.n
            tok_ln = emit_ln(tok_x_holder[0], sX if call_i == 0 else sV)

            def ffn_dump(buf, width, vtok):
                P.wait("gpsimd", sV, vtok)
                dt_ = P.dma("gpsimd", sLG, dbuf_d[:, 0:width], buf)
                P.wait("gpsimd", sLG, dt_)
                return dt_

            if MKFFN == "ln" and call_i == 0:
                return ffn_dump(xln[:], 16384, tok_ln), "g"

            P.wait("sync", sV, tok_ln)
            tok_spill = 0
            for tc in range(NTC):
                tok_spill = P.dma("sync", sX, x_spill[128 * tc:128 * (tc + 1), :],
                                  x_tm[:, 2048 * tc:2048 * (tc + 1)])

            P.wait("vector", sX, tok_spill)
            P.wait("tensor", sV, tok_ln)
            srcs = []
            for fc in range(16):
                for tc in range(NTC):
                    srcs.append(xln[:, 2048 * tc + 128 * fc:2048 * tc + 128 * (fc + 1)])

            def evy0(v, ps, b0, n):
                fc = b0 // NTC
                return v.tensor_scalar(y0[:, 1024 * fc + 128 * (b0 % NTC):
                                          1024 * fc + 128 * (b0 % NTC) + 128 * n],
                                       ps, su_fm[:, fc:fc + 1], None, ALU.mult)
            transpose_burst(srcs, evy0)

            for st_ in range(4):
                step = 1 << st_
                a_cnt = 16 // (2 * step)
                yx = y0[:].rearrange("p (a s b m) -> p a s b m", a=a_cnt, s=2, b=step)
                tx = btemp[:].rearrange("p (a b m) -> p a b m", a=a_cnt, b=step)
                P.op("vector", lambda v, yx=yx, tx=tx: v.tensor_sub(
                    tx[:, :, :, :], yx[:, :, 0, :, :], yx[:, :, 1, :, :]))
                P.op("vector", lambda v, yx=yx: v.tensor_add(
                    yx[:, :, 0, :, :], yx[:, :, 0, :, :], yx[:, :, 1, :, :]))
                P.op("vector", lambda v, yx=yx, tx=tx: v.tensor_copy(
                    yx[:, :, 1, :, :], tx[:, :, :, :]))
            tok_y1 = P.opi("vector", sV, lambda v: v.tensor_copy(y1[:], y0[:]))

            if MKFFN == "y1" and call_i == 0:
                return ffn_dump(y1[:], 16384, tok_y1), "g"

            P.wait("tensor", sV, tok_y1)
            P.wait("tensor", sCC, tok_ag)
            P.wait("sync", sCC, tok_ag)
            gate_done = {}
            wbuf_free = {}
            for mcp in range(32):
                rank, m2c = mcp // 4, mcp % 4
                bufp = mcp % 2
                if mcp >= 2:
                    P.wait("sync", sT, wbuf_free[mcp - 2])
                P.dma("sync", sWU, wub[bufp][:].rearrange('p (f m) -> p f m', f=16),
                      up_tile_view(rank, m2c, False))
                tokw = P.dma("sync", sWU, wgb[bufp][:].rearrange('p (f m) -> p f m', f=16),
                             up_tile_view(rank, m2c, True))
                last = None
                for sub in range(2):
                    mc = mcp * 2 + sub
                    if mc >= 1:
                        P.wait("tensor", sV, gate_done[(mc - 1) * 2])
                    else:
                        P.wait("tensor", sV, v0)
                    for i_ in range(4):
                        bank_acquire(i_)
                    P.wait("tensor", sWU, tokw)
                    first_gd1 = [True]
                    for fc in range(16):
                        wu = wub[bufp][:, 256 * fc + 128 * sub:256 * fc + 128 * (sub + 1)]
                        wg = wgb[bufp][:, 256 * fc + 128 * sub:256 * fc + 128 * (sub + 1)]
                        y0_ = y1[:, 1024 * fc:1024 * fc + 512]
                        y1_ = y1[:, 1024 * fc + 512:1024 * fc + 1024]
                        P.op("tensor", lambda t, w=wu, y=y0_, fc=fc: t.matmul(
                            pb[0][:], w, y, start=(fc == 0), stop=(fc == 15)))
                        if fc == 0 and mc >= 1:
                            P.wait("tensor", sV, gate_done[(mc - 1) * 2 + 1])
                        P.op("tensor", lambda t, w=wu, y=y1_, fc=fc: t.matmul(
                            pb[1][:], w, y, start=(fc == 0), stop=(fc == 15)))
                        P.op("tensor", lambda t, w=wg, y=y0_, fc=fc: t.matmul(
                            pb[2][:], w, y, start=(fc == 0), stop=(fc == 15)))
                        if fc < 15:
                            P.op("tensor", lambda t, w=wg, y=y1_, fc=fc: t.matmul(
                                pb[3][:], w, y, start=(fc == 0), stop=False))
                        else:
                            last = P.opi("tensor", sT, lambda t, w=wg, y=y1_: t.matmul(
                                pb[3][:], w, y, start=False, stop=True))
                    P.wait("scalar", sT, last)
                    sl0 = P.opi("scalar", sS, lambda s, mc=mc: s.activation(
                        silu_b[0][:], pb[2][:], AF.Silu, scale=sg_fm[:, mc:mc + 1]))
                    sl1 = P.opi("scalar", sS, lambda s, mc=mc: s.activation(
                        silu_b[1][:], pb[3][:], AF.Silu, scale=sg_fm[:, mc:mc + 1]))
                    P.wait("vector", sS, sl0)
                    P.wait("vector", sT, last)
                    gd0 = P.opi("vector", sV, lambda v, mc=mc: v.scalar_tensor_tensor(
                        htil[:, 1024 * mc:1024 * mc + 512],
                        pb[0][:], cu_fm[:, mc:mc + 1], silu_b[0][:], ALU.mult, ALU.mult))
                    P.wait("vector", sS, sl1)
                    gd1 = P.opi("vector", sV, lambda v, mc=mc: v.scalar_tensor_tensor(
                        htil[:, 1024 * mc + 512:1024 * mc + 1024],
                        pb[1][:], cu_fm[:, mc:mc + 1], silu_b[1][:], ALU.mult, ALU.mult))
                    gate_done[mc * 2] = gd0
                    gate_done[mc * 2 + 1] = gd1
                    for i_ in range(4):
                        bank_release(i_, sV, gd1)
                wbuf_free[mcp] = last
            tok_upgate = P.opi("vector", sV, lambda v: v.engine_nop())

            if MKFFN == "ht" and call_i == 0:
                return ffn_dump(htil[:, 0:16384], 16384, tok_upgate), "g"
            if MKFFN == "ht2" and call_i == 0:
                return ffn_dump(htil[:, 49152:65536], 16384, tok_upgate), "g"

            P.wait("sync", sCC, tok_agd)
            P.wait("tensor", sCC, tok_agd)
            P.wait("tensor", sV, tok_upgate)
            t_dn0 = sT.n
            down_done = {}
            dbuf_free = {}
            for r in range(NC):
                for gh in range(2):
                    bufp = gh
                    if r >= 1:
                        P.wait("sync", sT, dbuf_free[(r - 1) * 2 + gh])
                    else:
                        P.wait("sync", sT, t_dn0)
                    tokw = P.dma("sync", sWD, wdb[bufp][:].rearrange('p (g m) -> p g m', g=32),
                                 down_tile_view(r, gh))
                    if gh == 0:
                        for dlc in range(2):
                            for tt in range(2):
                                if r >= 1:
                                    P.wait("tensor", sV, down_done[(2 * (r - 1) + dlc) * 2 + tt])
                                else:
                                    P.wait("tensor", sV, tok_upgate)
                                bank_acquire(2 + dlc * 2 + tt)
                    P.wait("tensor", sWD, tokw)
                    lastmm = None
                    for gi in range(32):
                        gc = gh * 32 + gi
                        for dlc in range(2):
                            dc = 2 * r + dlc
                            w = wdb[bufp][:, 256 * gi + 128 * dlc:256 * gi + 128 * (dlc + 1)]
                            for tt in range(2):
                                par4 = 2 + dlc * 2 + tt
                                hsl = htil[:, 1024 * gc + 512 * tt:1024 * gc + 512 * (tt + 1)]
                                if gc == 63:
                                    lastmm = P.opi("tensor", sT,
                                                   lambda t, w=w, h=hsl, par4=par4: t.matmul(
                                                       pb[par4][:], w, h, start=False, stop=True))
                                    P.wait("vector", sT, lastmm)
                                    dd = P.opi("vector", sV,
                                               lambda v, par4=par4, dc=dc, tt=tt: v.tensor_scalar(
                                                   down_fm[:, 1024 * dc + 512 * tt:
                                                           1024 * dc + 512 * (tt + 1)],
                                                   pb[par4][:], wsd_fm[:, dc:dc + 1], None,
                                                   ALU.mult))
                                    down_done[dc * 2 + tt] = dd
                                    bank_release(par4, sV, dd)
                                elif gi == 31 and dlc == 1 and tt == 1:
                                    lastmm = P.opi("tensor", sT,
                                                   lambda t, w=w, h=hsl, par4=par4,
                                                   st_=(gc == 0): t.matmul(
                                                       pb[par4][:], w, h, start=st_, stop=False))
                                else:
                                    P.op("tensor", lambda t, w=w, h=hsl, par4=par4,
                                         st_=(gc == 0): t.matmul(
                                             pb[par4][:], w, h, start=st_, stop=False))
                    dbuf_free[r * 2 + gh] = lastmm
            tok_down = P.opi("vector", sV, lambda v: v.engine_nop())

            if MKFFN == "down" and call_i == 0:
                return ffn_dump(down_fm[:], 16384, tok_down), "g"

            P.wait("sync", sV, tok_down)
            tok_rel = 0
            for tc in range(NTC):
                tok_rel = P.dma("sync", sX, x_tm[:, 2048 * tc:2048 * (tc + 1)],
                                x_spill[128 * tc:128 * (tc + 1), :])

            P.wait("tensor", sV, tok_down)
            P.wait("vector", sX, tok_rel)
            srcs = []
            for tc in range(NTC):
                for dc in range(16):
                    srcs.append(down_fm[:, 1024 * dc + 128 * tc:1024 * dc + 128 * (tc + 1)])

            def evxb(v, ps, b0, n):
                tc, dc = b0 // 16, b0 % 16
                sl = x_tm[:, 2048 * tc + 128 * dc:2048 * tc + 128 * dc + 128 * n]
                return v.scalar_tensor_tensor(sl, ps, 0.5, sl, ALU.mult, ALU.add)
            transpose_burst(srcs, evxb)
            return P.opi("vector", sV, lambda v: v.engine_nop())

        if PH < 2:
            P.wait("sync", sLD, tok_x_holder[0])
            P.wait("sync", sCC, tok_ag)
            tok_st = 0
            for tc in range(NTC):
                tok_st = sync_load(out_d[128 * tc:128 * (tc + 1), :],
                                   x_tm[:, 2048 * tc:2048 * (tc + 1)])
            P.wait("sync", sLD, tok_st)
            with nc.Block() as blk:
                P.replay(blk)
            return nc

        tok_ffn1 = emit_ffn(0)
        if isinstance(tok_ffn1, tuple):
            with nc.Block() as blk:
                P.replay(blk)
            return nc
        tok_x_holder[0] = tok_ffn1

        if DBG:
            P.wait("sync", sV, tok_ffn1)
            for tc in range(NTC):
                sync_load(dbg1[128 * tc:128 * (tc + 1), :], x_tm[:, 2048 * tc:2048 * (tc + 1)])

        if PH < 3:
            P.wait("sync", sV, tok_ffn1)
            tok_st = 0
            for tc in range(NTC):
                tok_st = sync_load(out_d[128 * tc:128 * (tc + 1), :],
                                   x_tm[:, 2048 * tc:2048 * (tc + 1)])
            P.wait("sync", sLD, tok_st)
            with nc.Block() as blk:
                P.replay(blk)
            return nc

        tok_ln2 = emit_ln(tok_ffn1, sV)

        P.wait("tensor", sV, tok_ln2)
        srcs = []
        for fc in range(16):
            for tc in range(NTC):
                srcs.append(xln[:, 2048 * tc + 128 * fc:2048 * tc + 128 * (fc + 1)])

        def evx2(v, ps, b0, n):
            return v.tensor_copy(xln2[:, 128 * b0:128 * (b0 + n)], ps)
        tok_xln2 = transpose_burst(srcs, evx2)

        a1v = a1_in.ap().rearrange("(j e) -> j e", j=NC)

        def kq_dst(j, sl, isq):
            base = (SH_K if isq else 0) + sl * 128 * 1024
            return a1v[j, base:base + 128 * 1024].rearrange("(p m) -> p m", p=128)

        def v_dst(j, sl, tc):
            base = SH_K + SH_Q + sl * 8 * 128 * 128 + tc * 128 * 128
            return a1v[j, base:base + 128 * 128].rearrange("(p m) -> p m", p=128)

        P.wait("gpsimd", sV, tok_xln2)
        v_attn0 = sV.n
        ev_done = {}
        wq_free = {}
        kq_iter = 0
        for isq, (w_dram, bias_fm) in enumerate(((wk_d, bk_fm), (wq_d, bq_fm))):
            for oc in range(16):
                it = kq_iter
                kq_iter += 1
                bufp = it % 2
                if it >= 2:
                    P.wait("gpsimd", sT, wq_free[it - 2])
                wsrc = w_dram[:, 128 * oc:128 * (oc + 1)].rearrange("(f p) m -> p f m", p=128)
                tokw = cast_load(wqb[bufp][:].rearrange("p (f m) -> p f m", f=16), wsrc)
                P.wait("tensor", sLG, tokw)
                P.wait("tensor", sV, tok_xln2)
                bb = (it % 2) * 2
                if it >= 2 and (it - 2, 1) in ev_done:
                    P.wait("tensor", sV, ev_done[(it - 2, 1)])
                else:
                    P.wait("tensor", sV, v_attn0)
                bank_acquire(bb)
                bank_acquire(bb + 1)
                last = None
                for fc in range(16):
                    wsl = wqb[bufp][:, 128 * fc:128 * (fc + 1)]
                    P.op("tensor", lambda t, w=wsl, fc=fc, bb=bb: t.matmul(
                        pb[bb][:], w, xln2[:, 1024 * fc:1024 * fc + 512],
                        start=(fc == 0), stop=(fc == 15)))
                    if fc < 15:
                        P.op("tensor", lambda t, w=wsl, fc=fc, bb=bb: t.matmul(
                            pb[bb + 1][:], w, xln2[:, 1024 * fc + 512:1024 * fc + 1024],
                            start=(fc == 0), stop=False))
                    else:
                        last = P.opi("tensor", sT, lambda t, w=wsl, bb=bb: t.matmul(
                            pb[bb + 1][:], w, xln2[:, 1024 * fc + 512:1024 * fc + 1024],
                            start=False, stop=True))
                P.wait("vector", sT, last)
                j, sl = oc % 8, oc // 8
                for tt in range(2):
                    if it >= 1 and ("dma", it - 1, tt) in ev_done:
                        P.wait("vector", sWR, ev_done[("dma", it - 1, tt)])
                    ev = P.opi("vector", sV,
                               lambda v, bb=bb, tt=tt, oc=oc, bias_fm=bias_fm: v.tensor_scalar(
                                   qkv_ev[tt][:], pb[bb + tt][:], bias_fm[:, oc:oc + 1],
                                   None, ALU.add))
                    ev_done[(it, tt)] = ev
                    P.wait("sync", sV, ev)
                    dtok = P.dma("sync", sWR, kq_dst(j, sl, bool(isq))[:, 512 * tt:512 * (tt + 1)],
                                 qkv_ev[tt][:])
                    ev_done[("dma", it, tt)] = dtok
                bank_release(bb, sV, ev_done[(it, 0)])
                bank_release(bb + 1, sV, ev_done[(it, 1)])
                wq_free[it] = last
        P.wait("gpsimd", sT, sT.n)
        vv_done = {}
        wv_src = wv_d[:, :].rearrange("(f p) m -> p f m", p=128)
        for half in range(2):
            if half == 1:
                P.wait("gpsimd", sT, vv_done[(0, NTC - 1)])
            tok_wv = cast_load(wvb[0][:].rearrange("p (f m) -> p f m", f=16),
                               wv_src[:, :, 1024 * half:1024 * half + 512])
            tok_wv = cast_load(wvb[1][:].rearrange("p (f m) -> p f m", f=16),
                               wv_src[:, :, 1024 * half + 512:1024 * half + 1024])
            P.wait("tensor", sLG, tok_wv)
            for tc in range(NTC):
                key = half * NTC + tc
                bb = (key % 2) * 2
                bank_acquire(bb)
                bank_acquire(bb + 1)
                last = None
                for fc in range(16):
                    lh = xln2[:, 1024 * fc + 128 * tc:1024 * fc + 128 * (tc + 1)]
                    w0 = wvb[0][:, 512 * fc:512 * (fc + 1)]
                    w1 = wvb[1][:, 512 * fc:512 * (fc + 1)]
                    P.op("tensor", lambda t, lh=lh, w=w0, bb=bb, fc=fc: t.matmul(
                        pb[bb][:], lh, w, start=(fc == 0), stop=(fc == 15)))
                    if fc < 15:
                        P.op("tensor", lambda t, lh=lh, w=w1, bb=bb, fc=fc: t.matmul(
                            pb[bb + 1][:], lh, w, start=(fc == 0), stop=False))
                    else:
                        last = P.opi("tensor", sT, lambda t, lh=lh, w=w1, bb=bb: t.matmul(
                            pb[bb + 1][:], lh, w, start=False, stop=True))
                P.wait("vector", sT, last)
                evl = None
                for k2 in range(2):
                    ot = 2 * half + k2
                    if ("dma", key - 1, k2) in vv_done:
                        P.wait("vector", sWR, vv_done[("dma", key - 1, k2)])
                    evl = P.opi("vector", sV, lambda v, bb=bb, k2=k2: v.tensor_copy(
                        qkv_ev[k2][:], pb[bb + k2][:]))
                    P.wait("sync", sV, evl)
                    dtok = 0
                    for k4 in range(4):
                        oc = 4 * ot + k4
                        j, sl = oc % 8, oc // 8
                        dtok = P.dma("sync", sWR, v_dst(j, sl, tc),
                                     qkv_ev[k2][:, 128 * k4:128 * (k4 + 1)])
                    vv_done[("dma", key, k2)] = dtok
                vv_done[(half, tc)] = last
                bank_release(bb, sV, sV.n)
                bank_release(bb + 1, sV, sV.n)

        def _a2a(g, i_, o_):
            with allow_ici_all_to_all():
                return g.collective_compute("AllToAll", ALU.bypass,
                                            replica_groups=[list(range(NC))],
                                            ins=[i_.ap().opt()], outs=[o_.ap().opt()])

        tok_a1w = sWR.n
        P.wait("gpsimd", sWR, tok_a1w)
        tok_a2a1 = P.opi("gpsimd", sCC, lambda g: _a2a(g, a1_in, a1_out))

        if PH < 4:
            P.wait("sync", sCC, tok_a2a1)
            P.wait("sync", sV, tok_xln2)
            tok_st = 0
            for tc in range(NTC):
                tok_st = sync_load(out_d[128 * tc:128 * (tc + 1), :],
                                   x_tm[:, 2048 * tc:2048 * (tc + 1)])
            P.wait("sync", sLD, tok_st)
            with nc.Block() as blk:
                P.replay(blk)
            return nc

        a1o = a1_out.ap().rearrange("(r e) -> r e", r=NC)

        def kq_src(rank, sl, isq):
            base = (SH_K if isq else 0) + sl * 128 * 1024
            return a1o[rank, base:base + 128 * 1024].rearrange("(p m) -> p m", p=128)

        def v_src(rank, sl):
            base = SH_K + SH_Q + sl * 8 * 128 * 128
            return a1o[rank, base:base + 8 * 128 * 128].rearrange("(t p m) -> p t m", t=8, p=128)

        a2v = a2_in.ap().rearrange("(s e) -> s e", s=NC)

        def o_dst(s, upos, qhalf):
            blk = a2v[s, upos * 128 * 1024:(upos + 1) * 128 * 1024].rearrange("(p m) -> p m", p=128)
            return blk[:, 512 * qhalf:512 * (qhalf + 1)]

        P.wait("sync", sV, tok_ffn1)
        tok_m4 = sync_load(M4_t[:], M4_c[:])
        P.wait("vector", sLD, tok_m4)

        P.wait("sync", sCC, tok_a2a1)
        P.wait("tensor", sCC, tok_a2a1)
        v_ai0 = sV.n
        s_ai0 = sS.n
        sc_free = {}
        av_free = {}
        z_free = {}
        tmp_free = {}
        ab_free = {}
        krow_free = {}
        qsb_free = {}
        rk_toks = {}
        qt_toks = {}
        KSLOT = 6
        kslot_ctr = [0]
        du_tok = [0, 0]
        blkctr = 0

        UQ = []
        for u in range(4):
            for qt in range(8):
                UQ.append((u, qt))

        def emit_du(u):
            sl_u = u // 2
            P.wait("vector", sS, sS.n)
            P.opi("vector", sV, lambda v, sl_u=sl_u: v.tensor_scalar(
                Du[:, 0:512], D_t[:], slopes_t[:, sl_u:sl_u + 1], None, ALU.mult))
            for mi in range(4):
                P.op("vector", lambda v, mi=mi: v.tensor_tensor(
                    Du[:, 512 * (mi + 1):512 * (mi + 2)], Du[:, 0:512],
                    M4_t[:, 512 * mi:512 * (mi + 1)], ALU.add))
            P.op("vector", lambda v, sl_u=sl_u: v.tensor_scalar(
                biasv[:], drow_t[:], slopes_t[:, sl_u:sl_u + 1], None, ALU.mult))
            du_tok[0] = P.opi("vector", sV, lambda v: v.engine_nop())

        def emit_loads(uq, max_rows=None):
            u, qt = UQ[uq]
            b_u, sl_u = u % 2, u // 2
            qb = uq % 2
            lo, hi = kb_range(qt, sl_u)
            rows = list(range(lo // 8, hi // 8 + 1))
            if uq not in qt_toks:
                if uq >= 2 and (uq - 2) in qsb_free:
                    P.wait("sync", sT, qsb_free[uq - 2])
                qt_toks[uq] = P.dma("sync", sQL, q_sb[qb][:],
                                    kq_src(4 * b_u + qt // 2, sl_u, True)[
                                        :, 512 * (qt % 2):512 * (qt % 2 + 1)])
            n_emit = 0
            for rk in rows:
                if (uq, rk) in rk_toks:
                    continue
                if max_rows is not None and n_emit >= max_rows:
                    break
                sl_i = kslot_ctr[0] % KSLOT
                kslot_ctr[0] += 1
                if ("slot", sl_i) in krow_free:
                    P.wait("sync", sT, krow_free[("slot", sl_i)])
                P.dma("sync", sKV, k_sb[sl_i][:], kq_src(4 * b_u + rk, sl_u, False))
                rk_toks[(uq, rk)] = (sl_i, P.dma(
                    "sync", sKV, v_sb[sl_i][:].rearrange("p (t m) -> p t m", t=8),
                    v_src(4 * b_u + rk, sl_u)))
                n_emit += 1

        emit_du(0)
        for uq in range(len(UQ)):
            u, qt = UQ[uq]
            b_u, sl_u = u % 2, u // 2
            qb = uq % 2
            lo, hi = kb_range(qt, sl_u)
            pav = 3 + uq % 2
            zb = 2 if uq % 2 == 0 else 5
            P.wait("tensor", sV, av_free.get(pav, v_ai0))
            first = True
            lastq = None
            lastt = None
            kbs = list(range(lo, hi + 1))
            kbs2rows = list(range(lo // 8, hi // 8 + 1))
            emit_loads(uq)
            DEP = 3
            st_info = {}
            seen_rk = set()
            emitted_next = False
            for i in range(len(kbs) + DEP):
                if i >= 2 and not emitted_next:
                    emitted_next = True
                    if uq + 1 < len(UQ):
                        emit_loads(uq + 1, max_rows=KSLOT - len(kbs2rows))
                if i < len(kbs):
                    kb = kbs[i]
                    rslot = rk_toks[(uq, kb // 8)][0]
                    col = kb % 8
                    ps = blkctr % 2
                    tb = blkctr % 4
                    P.wait("tensor", sV, sc_free.get(ps, v_ai0))
                    if kb // 8 not in seen_rk:
                        P.wait("tensor", sKV, rk_toks[(uq, kb // 8)][1])
                        if not seen_rk:
                            P.wait("tensor", sQL, qt_toks[uq])
                        seen_rk.add(kb // 8)
                    ts = P.opi("tensor", sT,
                               lambda t, rslot=rslot, col=col, qb=qb, ps=ps: t.matmul(
                                   pb[ps][:], k_sb[rslot][:, 128 * col:128 * (col + 1)],
                                   q_sb[qb][:], start=True, stop=True))
                    lastq = ts
                    midx = 4 * qt - kb + 3
                    dsl = (midx + 1) if 0 <= midx <= 3 else 0
                    P.wait("vector", sT, ts)
                    P.wait("vector", sV, du_tok[0])
                    P.wait("vector", sS, tmp_free.get(tb, 0))
                    tv = P.opi("vector", sV,
                               lambda v, tb=tb, ps=ps, dsl=dsl: v.scalar_tensor_tensor(
                                   tmpS[tb][:], pb[ps][:], RSQ_HD,
                                   Du[:, 512 * dsl:512 * (dsl + 1)], ALU.mult, ALU.add))
                    sc_free[ps] = tv
                    P.wait("scalar", sV, tv)
                    P.wait("scalar", sT, ab_free.get(tb, 0))
                    if tb == 0 and ab_free.get("odma", 0):
                        P.wait("scalar", sWR, ab_free["odma"])
                    te = P.opi("scalar", sS, lambda s, tb=tb, midx=midx: s.activation(
                        abuf[tb][:], tmpS[tb][:], AF.Exp,
                        bias=biasv[:, midx:midx + 1], scale=1.0))
                    tmp_free[tb] = te
                    st_info[i] = (kb, rslot, col, tb, te)
                    blkctr += 1
                j = i - DEP
                if j >= 0:
                    kbj, rslotj, colj, tbj, tej = st_info.pop(j)
                    lastj = (kbj == hi)
                    P.wait("tensor", sS, tej)
                    if first:
                        P.wait("tensor", sV, z_free.get(zb, v_ai0))
                    P.op("tensor", lambda t, rslot=rslotj, col=colj, tb=tbj, pav=pav,
                         first=first, last=lastj: t.matmul(
                             pb[pav][:], v_sb[rslot][:, 128 * col:128 * (col + 1)],
                             abuf[tb][:], start=first, stop=last))
                    tz = P.opi("tensor", sT, lambda t, tb=tbj, zb=zb, first=first,
                               last=lastj: t.matmul(
                                   pb[zb][0:1, :], ones_bf[:], abuf[tb][:],
                                   start=first, stop=last))
                    ab_free[tbj] = tz
                    lastt = tz
                    first = False
            for rk in range(lo // 8, hi // 8 + 1):
                krow_free[("slot", rk_toks[(uq, rk)][0])] = lastt
            qsb_free[uq] = lastq
            P.wait("vector", sT, lastt)
            tz1 = P.opi("vector", sV, lambda v, zb=zb: v.tensor_copy(z_sb[:], pb[zb][0:1, :]))
            P.wait("vector", sV, tz1)
            tz2 = P.opi("vector", sV, lambda v: v.reciprocal(rz_sb[:], z_sb[:]))
            P.wait("tensor", sV, tz2)
            tbq = P.opi("tensor", sT, lambda t, zb=zb: t.matmul(
                pb[zb][:], ones_f[:], rz_sb[:], start=True, stop=True))
            P.wait("vector", sT, tbq)
            tz3 = P.opi("vector", sV, lambda v, zb=zb: v.tensor_copy(rzb[:], pb[zb][:]))
            z_free[zb] = tz3
            P.wait("vector", sV, tz3)
            P.wait("vector", sS, sS.n)
            tz4 = P.opi("vector", sV, lambda v, pav=pav: v.tensor_tensor(
                tmpS[0][:], pb[pav][:], rzb[:], ALU.mult))
            av_free[pav] = tz4
            P.wait("vector", sV, tz4)
            tz5 = P.opi("vector", sV, lambda v, sl_u=sl_u: v.tensor_scalar(
                abuf[0][:], tmpS[0][:], bvT_t[:, sl_u:sl_u + 1], None, ALU.add))
            P.wait("sync", sV, tz5)
            otok = P.dma("sync", sWR, o_dst(4 * b_u + qt // 2, u // 2, qt % 2), abuf[0][:])
            P.wait("vector", sWR, otok)
            ab_free["odma"] = otok
            if uq + 1 < len(UQ) and UQ[uq + 1][0] != u:
                emit_du(UQ[uq + 1][0])

        tok_a2w = sWR.n
        P.wait("gpsimd", sWR, tok_a2w)
        tok_a2a2 = P.opi("gpsimd", sCC, lambda g: _a2a(g, a2_in, a2_out))

        if PH < 5:
            P.wait("sync", sCC, tok_a2a2)
            tok_st = 0
            for tc in range(NTC):
                tok_st = sync_load(out_d[128 * tc:128 * (tc + 1), :],
                                   x_tm[:, 2048 * tc:2048 * (tc + 1)])
            P.wait("sync", sLD, tok_st)
            with nc.Block() as blk:
                P.replay(blk)
            return nc

        v_bo0 = sV.n
        for ch in range(4):
            tok_bo = sync_load(rz_sb[:, 0:512], bo_d[:, 512 * ch:512 * (ch + 1)])
            P.wait("tensor", sLD, tok_bo)
            P.wait("tensor", sV, v_bo0)
            bank_acquire(5)
            tb_ = P.opi("tensor", sT, lambda t: t.matmul(
                pb[5][:], ones_f[:], rz_sb[:], start=True, stop=True))
            P.wait("vector", sT, tb_)
            ev = P.opi("vector", sV, lambda v, ch=ch: v.tensor_copy(
                bo_b[:, 512 * ch:512 * (ch + 1)], pb[5][:]))
            bank_release(5, sV, ev)
            P.wait("sync", sV, ev)
            P.wait("tensor", sV, ev)
        for tc in range(NTC):
            P.op("vector", lambda v, tc=tc: v.tensor_add(
                x_tm[:, 2048 * tc:2048 * (tc + 1)], x_tm[:, 2048 * tc:2048 * (tc + 1)], bo_b[:]))
        tok_xbo = P.opi("vector", sV, lambda v: v.engine_nop())

        a2o = a2_out.ap().rearrange("(r e) -> r e", r=NC)
        P.wait("sync", sCC, tok_a2a2)
        P.wait("sync", sV, tok_xbo)
        tok_og = 0
        for hc in range(16):
            rank, slot = hc % 8, hc // 8
            src = a2o[rank, slot * 128 * 1024:(slot + 1) * 128 * 1024].rearrange(
                "(p m) -> p m", p=128)
            tok_og = P.dma("sync", sOG, ogath[:, 1024 * hc:1024 * (hc + 1)], src)

        P.wait("tensor", sOG, tok_og)
        P.wait("tensor", sV, tok_xbo)
        P.wait("gpsimd", sT, sT.n)
        v_op0 = sV.n
        op_done = {}
        wo_src = wo_d[:, :].rearrange("(f p) m -> p f m", p=128)
        for half in range(2):
            if half == 1:
                P.wait("gpsimd", sT, op_done[(0, NTC - 1)])
            tok_wo = cast_load(wob[0][:].rearrange("p (f m) -> p f m", f=16),
                               wo_src[:, :, 1024 * half:1024 * half + 512])
            tok_wo = cast_load(wob[1][:].rearrange("p (f m) -> p f m", f=16),
                               wo_src[:, :, 1024 * half + 512:1024 * half + 1024])
            P.wait("tensor", sLG, tok_wo)
            for tc in range(NTC):
                key = half * NTC + tc
                bb = (key % 2) * 2
                bank_acquire(bb)
                bank_acquire(bb + 1)
                last = None
                for ic in range(16):
                    lh = ogath[:, 1024 * ic + 128 * tc:1024 * ic + 128 * (tc + 1)]
                    w0 = wob[0][:, 512 * ic:512 * (ic + 1)]
                    w1 = wob[1][:, 512 * ic:512 * (ic + 1)]
                    P.op("tensor", lambda t, lh=lh, w=w0, bb=bb, ic=ic: t.matmul(
                        pb[bb][:], lh, w, start=(ic == 0), stop=(ic == 15)))
                    if ic < 15:
                        P.op("tensor", lambda t, lh=lh, w=w1, bb=bb, ic=ic: t.matmul(
                            pb[bb + 1][:], lh, w, start=(ic == 0), stop=False))
                    else:
                        last = P.opi("tensor", sT, lambda t, lh=lh, w=w1, bb=bb: t.matmul(
                            pb[bb + 1][:], lh, w, start=False, stop=True))
                P.wait("vector", sT, last)
                ev = None
                for k2 in range(2):
                    ot = 2 * half + k2
                    ev = P.opi("vector", sV, lambda v, bb=bb, k2=k2, tc=tc, ot=ot: v.tensor_add(
                        x_tm[:, 2048 * tc + 512 * ot:2048 * tc + 512 * (ot + 1)],
                        x_tm[:, 2048 * tc + 512 * ot:2048 * tc + 512 * (ot + 1)], pb[bb + k2][:]))
                bank_release(bb, sV, ev)
                bank_release(bb + 1, sV, ev)
                op_done[(half, tc)] = last
        tok_attn = P.opi("vector", sV, lambda v: v.engine_nop())
        tok_x_holder[0] = tok_attn

        if DBG:
            P.wait("sync", sV, tok_attn)
            for tc in range(NTC):
                sync_load(dbg2[128 * tc:128 * (tc + 1), :], x_tm[:, 2048 * tc:2048 * (tc + 1)])

        if PH < 6:
            P.wait("sync", sV, tok_attn)
            tok_st = 0
            for tc in range(NTC):
                tok_st = sync_load(out_d[128 * tc:128 * (tc + 1), :],
                                   x_tm[:, 2048 * tc:2048 * (tc + 1)])
            P.wait("sync", sLD, tok_st)
            with nc.Block() as blk:
                P.replay(blk)
            return nc

        tok_ffn2 = emit_ffn(1)

        P.wait("sync", sV, tok_ffn2)
        tok_st = 0
        for tc in range(NTC):
            tok_st = sync_load(out_d[128 * tc:128 * (tc + 1), :],
                               x_tm[:, 2048 * tc:2048 * (tc + 1)])
        P.wait("sync", sLD, tok_st)

        with nc.Block() as blk:
            P.replay(blk)

    return nc



def kernel(**inputs):
    flags = os.environ.get("AXON_NCC_FLAGS", "")
    if "--enable-ldw-opt=false" in flags:
        os.environ["AXON_NCC_FLAGS"] = flags.replace("--enable-ldw-opt=false",
                                                     "--enable-ldw-opt=true")
    x = np.ascontiguousarray(inputs["L0_attn_x"])
    nc = build_nc()
    slopes = np.array(SLOPES, dtype=np.float32)
    bv = np.asarray(inputs["L0_attn_bv"])
    wscale_ug = np.asarray(inputs["L0_ffn_Wscale_ug"])
    in_maps = []
    for c in range(NC):
        g, qr = c // 4, c % 4
        m = {
            "x_loc": np.ascontiguousarray(x[g, 1024 * qr:1024 * (qr + 1), :]),
            "wq": inputs["L0_attn_wq"], "wk": inputs["L0_attn_wk"],
            "wv": inputs["L0_attn_wv"], "wo": inputs["L0_attn_wo"],
            "wup_loc": np.ascontiguousarray(inputs["L0_ffn_W_up"][1024 * c:1024 * (c + 1), :]),
            "wgate_loc": np.ascontiguousarray(inputs["L0_ffn_W_gate"][1024 * c:1024 * (c + 1), :]),
            "wdown_loc": np.ascontiguousarray(inputs["L0_ffn_W_down"][256 * c:256 * (c + 1), :]),
            "bq": np.asarray(inputs["L0_attn_bq"]).reshape(1, D),
            "bk": np.asarray(inputs["L0_attn_bk"]).reshape(1, D),
            "bo": np.asarray(inputs["L0_attn_bo"]).reshape(1, D),
            "bvT": np.stack([bv[128 * c:128 * (c + 1)],
                             bv[128 * (c + 8):128 * (c + 9)]], axis=1).astype(np.float32),
            "slopes_b": np.broadcast_to(np.array([slopes[c], slopes[c + 8]],
                                                 dtype=np.float32)[None, :], (128, 2)).copy(),
            "su_ug": np.asarray(inputs["L0_ffn_SU_ug"]).reshape(1, D),
            "su_dp": np.asarray(inputs["L0_ffn_SU_dp"]).reshape(1, I),
            "wsu": wscale_ug[:I].reshape(1, I),
            "wsg": wscale_ug[I:].reshape(1, I),
            "wsd": np.asarray(inputs["L0_ffn_Wscale_dp"]).reshape(1, D),
        }
        in_maps.append({k: np.ascontiguousarray(v).astype(np.float32) for k, v in m.items()})

    res = run_bass_kernel_spmd(nc, in_maps, core_ids=list(range(NC)))
    global _dbg
    _dbg = res.results
    out = np.empty((B, S, D), dtype=np.float32)
    for c in range(NC):
        g, qr = c // 4, c % 4
        out[g, 1024 * qr:1024 * (qr + 1), :] = res.results[c]["out_loc"]
    return out


_dbg = None
